# version 02
# baseline (speedup 1.0000x reference)
import numpy as np
import ml_dtypes
import concourse.bass as bass
import concourse.mybir as mybir
from concourse.bass_utils import run_bass_kernel_spmd

F32 = mybir.dt.float32
BF16 = mybir.dt.bfloat16
AF = mybir.ActivationFunctionType
ALU = mybir.AluOpType
BF = ml_dtypes.bfloat16

B, T, D = 4, 8192, 1024
NH = 8
KD = 64
HC = NH * KD
SEG = 512
L = 64
NCH = SEG // L
TOWN = T // 2

ENGINES = ("tensor", "vector", "scalar", "sync", "gpsimd")


class Sem:
    def __init__(self, handle):
        self.h = handle
        self.v = 0

    def inc(self, n=1):
        self.v += n
        return self.v


class Ctx:
    def __init__(self, nc):
        self.nc = nc
        self.streams = {e: [] for e in ENGINES}
        self._sem_guards = []

    def sem(self, name):
        g = self.nc.semaphore(name)
        h = g.__enter__()
        self._sem_guards.append(g)
        return Sem(h)

    def op(self, eng, fn):
        self.streams[eng].append(fn)

    def inc_last(self, eng, sem, n=1):
        fn = self.streams[eng].pop()
        self.streams[eng].append(lambda e, fn=fn: fn(e).then_inc(sem.h, n))

    def wait(self, eng, sem, val):
        self.op(eng, lambda e, s=sem.h, v=val: e.wait_ge(s, v))

    def play(self, block):
        for name in ENGINES:
            ops = self.streams[name]
            if not ops:
                continue
            def body(eng, ops=ops):
                for f in ops:
                    f(eng)
            getattr(block, name)(body)
        self.streams = {e: [] for e in ENGINES}


def ap3(t, part_off, npart, dims, off_elems=0):
    h = t.handle if hasattr(t, "handle") else t
    fs = t.shape[1]
    return bass.AP(h, part_off * fs + off_elems, [[fs, npart]] + list(dims))



def np_delta_ref(q, k, v, g, beta):
    Tn, kd = q.shape
    S = np.zeros((kd, kd))
    o = np.zeros_like(q)
    for t in range(Tn):
        kn = k[t] / max(np.linalg.norm(k[t]), 1e-12)
        qt = q[t] * kd ** (-0.5)
        Sk = S.T @ kn
        delta = np.outer(kn, v[t] - Sk)
        S = np.exp(g[t]) * S + beta[t] * delta
        o[t] = S.T @ qt
    return o


def np_delta_chunked(q, k, v, g, beta, Lc=L):
    Tn, kd = q.shape
    qh = q * kd ** (-0.5)
    S = np.zeros((kd, kd))
    o = np.zeros_like(q)
    for c0 in range(0, Tn, Lc):
        sl = slice(c0, c0 + Lc)
        K = k[sl]; Q = qh[sl]; V = v[sl]
        sp = -g[sl]
        P = np.cumsum(sp)
        b = np.exp(-P)
        Pex = P - sp
        bex = np.exp(-Pex)
        ib = np.exp(P)
        n = 1.0 / np.maximum(np.sqrt((K * K).sum(-1)), 1e-12)
        psi = ib * beta[sl] * n
        sm = psi * bex * n
        Gkk = K @ K.T
        Gqk = Q @ K.T
        X = -(sm[:, None] * Gkk) * (np.arange(Lc)[:, None] > np.arange(Lc)[None, :])
        Rh = psi[:, None] * V - sm[:, None] * (K @ S)
        U = Rh.copy()
        Xp = X.copy()
        for _ in range(6):
            U = U + Xp @ U
            Xp = Xp @ Xp
        Ot = b[:, None] * (Q @ S + (Gqk * (np.arange(Lc)[:, None] >= np.arange(Lc)[None, :])) @ U)
        S = b[-1] * (S + K.T @ U)
        o[sl] = Ot
    return o


if __name__ == "__main__":
    rng = np.random.default_rng(0)
    Tn = 256
    q = rng.standard_normal((Tn, KD)) * 0.5
    k = rng.standard_normal((Tn, KD)) * 0.5
    v = rng.standard_normal((Tn, KD)) * 0.5
    zg = rng.standard_normal(Tn) * 0.7
    zb = rng.standard_normal(Tn) * 0.7
    g = -np.log1p(np.exp(-zg))
    beta = 1.0 / (1.0 + np.exp(-zb))
    o1 = np_delta_ref(q, k, v, g, beta)
    o2 = np_delta_chunked(q, k, v, g, beta)
    print("chunked vs naive maxerr:", np.abs(o1 - o2).max(), " rel:",
          np.abs(o1 - o2).max() / np.abs(o1).max())



class DeltaBufs:
    def __init__(self, nc):
        sb = nc.alloc_sbuf_tensor
        self.kf = sb("d_kf", [64, NH * SEG], BF16)
        self.qf = sb("d_qf", [64, NH * SEG], BF16)
        self.vf = sb("d_vf", [64, NH * SEG], BF16)
        self.gw = sb("d_gw", [32, 19 * SEG], F32)
        self.q3t = sb("d_q3t", [64, NCH * 24], F32)
        self.blb = sb("d_blb", [64, NH * NCH], F32)
        self.bend = sb("d_bend", [8, NCH], F32)
        self.xall = sb("d_xall", [64, NH * 64], BF16)
        self.y0 = sb("d_y0", [64, NH * 64], BF16)
        self.ybuf = [sb("d_yb%d" % i, [64, NH * 64], BF16) for i in range(2)]
        self.zbuf = [sb("d_zb%d" % i, [64, NH * 64], BF16) for i in range(2)]
        self.ubuf = [sb("d_ub%d" % i, [64, NH * 64], BF16) for i in range(2)]
        self.kqgm = [sb("d_kqgm%d" % i, [64, NH * 64], BF16) for i in range(2)]
        self.vt = sb("d_vt", [64, NH * 64], BF16)
        self.kt = [sb("d_kt%d" % i, [64, NH * 64], BF16) for i in range(2)]
        self.st = [sb("d_st%d" % i, [64, NH * 64], BF16) for i in range(2)]
        self.osb = sb("d_osb", [64, NH * 64], BF16)
        self.grsb = sb("d_grsb", [64, NH * 64], F32)
        self.kqraw = sb("d_kqraw", [64, NH * 64], F32)
        self.xraw = sb("d_xraw", [64, NH * 64], F32)
        self.rsc = sb("d_rsc", [64, NH * 64], F32)
        self.of = [sb("d_of%d" % i, [128, SEG], BF16) for i in range(4)]
        self.ksq2 = sb("d_ksq2", [64, NH * SEG], BF16)
        ps = nc.alloc_psum_tensor
        self.pA = ps("d_pA", [128, 512], F32)
        self.pB = ps("d_pB", [128, 512], F32)
        self.pV = ps("d_pV", [64, 1024], BF16)
        self.pE = ps("d_pE", [128, 512], F32)
        self.pF = ps("d_pF", [128, 512], F32)
        self.pU = ps("d_pU", [128, 512], F32)
        self.pS = ps("d_pS", [128, 512], F32)
        self.bXO = ps("d_bXO", [128, 512], F32)
        self.pDx = self.bXO[:, 0:128].bitcast(BF16)
        self.pDo = self.bXO[:, 128:256].bitcast(BF16)
        self.pBlb = self.bXO[0:64, 256:256 + NH * NCH]


class Consts:
    WIDTH = 992

    def __init__(self, nc):
        self.bf = nc.alloc_sbuf_tensor("cst_bf", [128, self.WIDTH], BF16, side="right")
        self.f24 = nc.alloc_sbuf_tensor("cst_f24", [24, 24], F32, side="right")
        self.oh32 = nc.alloc_sbuf_tensor("cst_oh32", [8, 512], F32, side="right")
        self.i8d = nc.alloc_sbuf_tensor("cst_i8d", [128, 8], F32, side="right")

    @staticmethod
    def host_tile():
        c = np.zeros((128, Consts.WIDTH), dtype=np.float32)
        c[:, 0:128] = np.eye(128)
        i, j = np.mgrid[0:64, 0:64]
        c[0:64, 128:192] = -(j < i).astype(np.float32)
        c[0:64, 192:256] = (i <= j).astype(np.float32)
        for h in range(NH):
            c[h, 256 + h * 64:256 + (h + 1) * 64] = 1.0
        c[0:24, 768:792] = np.eye(24)
        for h in range(8):
            c[0:64, 792 + h * 8 + h] = 1.0
        for base in (0, 32, 64, 96):
            for i2 in range(8):
                c[base + i2, 856 + i2] = 1.0
        c[:, 864:992] = 1.0 / 1024.0
        return c

    def I128(self): return self.bf[:, 0:128]
    def lnones(self): return self.bf[:, 864:992]
    def I64(self): return self.bf[0:64, 0:64]
    def maskLneg(self): return self.bf[0:64, 128:192]
    def maskU(self): return self.bf[0:64, 192:256]
    def onehot(self, h): return self.oh32[0:8, h * 64:(h + 1) * 64]
    def colones(self, h): return self.bf[0:64, 792 + h * 8:800 + h * 8]
    def i8(self): return self.i8d[0:8, :]


def hs0(t, h, c0, n):
    return t[0:64, h * SEG + c0:h * SEG + c0 + n]


def bcast_h(t, part0, nparts, col0, colstep, ncols, rep):
    h = t.handle if hasattr(t, "handle") else t
    fs = t.shape[1]
    return bass.AP(h, part0 * fs + col0, [[fs, nparts], [colstep, ncols], [0, rep]])


class DeltaEmitter:

    def __init__(self, ctx, bufs, cst):
        self.ctx = ctx
        self.b = bufs
        self.c = cst
        s = ctx.sems
        self.sP = s["dP"]; self.sV = s["dV"]; self.sA = s["dA"]; self.sG = s["dG"]
        self.ev = {}

    def emit_gates(self, z_src, kf_wait=None, z_wait=None):
        ctx, b, cst = self.ctx, self.b, self.c
        gw = b.gw
        sP, sV, sA, sG = self.sP, self.sV, self.sA, self.sG
        ev = self.ev

        def blk(i, n=1):
            return gw[0:8, i * SEG:(i + n) * SEG]

        if z_wait is not None:
            ctx.wait("scalar", z_wait[0], z_wait[1])
        if z_src is None:
            ctx.op("scalar", lambda e: e.activation(blk(2, 2), blk(0, 2), AF.Exp, scale=-1.0).then_inc(sA.h, 1))
        else:
            zg, zb = z_src
            ctx.op("scalar", lambda e, zg=zg: e.activation(blk(2), zg, AF.Exp, scale=-1.0))
            ctx.op("scalar", lambda e, zb=zb: e.activation(blk(3), zb, AF.Exp, scale=-1.0).then_inc(sA.h, 1))
        vW = sA.inc()
        ev["zw"] = vW
        ctx.wait("vector", sA, vW)
        ctx.op("vector", lambda e: e.tensor_scalar_add(blk(4, 2), blk(2, 2), 1.0).then_inc(sV.h, 1))
        vU = sV.inc()
        ctx.wait("scalar", sV, vU)
        ctx.op("scalar", lambda e: e.activation(blk(6, 2), blk(4, 2), AF.Ln).then_inc(sA.h, 1))
        vLn = sA.inc()
        ctx.wait("scalar", sA, vLn)
        ctx.op("scalar", lambda e: e.activation(blk(13), blk(7), AF.Exp, scale=-1.0).then_inc(sA.h, 1))
        vSp = sA.inc()
        ctx.wait("vector", sA, vLn)
        for c in range(NCH):
            ctx.op("vector", lambda e, c=c: e.tensor_tensor_scan(
                gw[0:8, 8 * SEG + c * L:8 * SEG + (c + 1) * L],
                gw[0:8, 6 * SEG + c * L:6 * SEG + (c + 1) * L],
                gw[0:8, 6 * SEG + c * L:6 * SEG + (c + 1) * L], 0.0, ALU.add, ALU.bypass))
        ctx.inc_last("vector", sV)
        vScan = sV.inc()
        ctx.wait("vector", sV, vScan)
        ctx.op("vector", lambda e: e.tensor_tensor(blk(9), blk(8), blk(6), ALU.subtract).then_inc(sV.h, 1))
        vPex = sV.inc()
        ctx.wait("scalar", sV, vPex)
        ctx.op("scalar", lambda e: e.activation(blk(10), blk(8), AF.Exp, scale=-1.0))
        ctx.op("scalar", lambda e: e.activation(blk(11), blk(8), AF.Exp))
        ctx.op("scalar", lambda e: e.activation(blk(12), blk(9), AF.Exp, scale=-1.0).then_inc(sA.h, 1))
        vExp = sA.inc()
        ksq2 = b.ksq2
        if kf_wait is not None:
            ctx.wait("scalar", kf_wait[0], kf_wait[1])
        if "rhat1" in ev:
            ctx.wait("tensor", sV, ev["rhat1"])
        if "ksq_last" in ev:
            ctx.wait("scalar", sP, ev["ksq_last"])
        for half in range(2):
            ctx.op("scalar", lambda e, half=half: e.activation(
                ksq2[:, half * 4 * SEG:(half + 1) * 4 * SEG],
                b.kf[:, half * 4 * SEG:(half + 1) * 4 * SEG],
                AF.Square).then_inc(sA.h, 1))
            vSq = sA.inc()
            ctx.wait("tensor", sA, vSq)
            for hh in range(4):
                h = half * 4 + hh
                ctx.op("tensor", lambda e, h=h: e.matmul(
                    b.pF[0:8, 0:SEG], cst.colones(h),
                    ksq2[:, h * SEG:(h + 1) * SEG],
                    start=(h == 0), stop=(h == 7)))
        ctx.inc_last("tensor", sP)
        vKsq = sP.inc()
        ev["ksq_last"] = vKsq
        ctx.wait("vector", sP, vKsq)
        ctx.op("vector", lambda e: e.tensor_scalar_max(blk(15), b.pF[0:8, 0:SEG], 1e-24).then_inc(sV.h, 1))
        vMx = sV.inc()
        ev["rhat1"] = vMx
        ctx.wait("scalar", sV, vMx)
        ctx.op("scalar", lambda e: e.activation(blk(16), blk(15), AF.Ln).then_inc(sA.h, 1))
        vLn2 = sA.inc()
        ctx.wait("scalar", sA, vLn2)
        ctx.op("scalar", lambda e: e.activation(blk(14), blk(16), AF.Exp, scale=-0.5).then_inc(sA.h, 1))
        vN = sA.inc()
        ctx.wait("vector", sA, vN)
        ctx.op("vector", lambda e: e.tensor_tensor(blk(15), blk(11), blk(13), ALU.mult).then_inc(sV.h, 1))
        v1 = sV.inc()
        ctx.wait("vector", sV, v1)
        ctx.op("vector", lambda e: e.tensor_tensor(blk(17), blk(15), blk(14), ALU.mult).then_inc(sV.h, 1))
        v2 = sV.inc()
        ctx.op("vector", lambda e: e.tensor_tensor(blk(16), blk(12), blk(14), ALU.mult).then_inc(sV.h, 1))
        v3 = sV.inc()
        ctx.wait("vector", sV, v3)
        ctx.op("vector", lambda e: e.tensor_tensor(blk(18), blk(16), blk(17), ALU.mult).then_inc(sV.h, 1))
        sV.inc()
        gwh = gw.handle if hasattr(gw, "handle") else gw
        ctx.op("vector", lambda e: e.tensor_copy(
            b.bend[0:8, 0:NCH],
            bass.AP(gwh, 10 * SEG + (L - 1), [[19 * SEG, 8], [L, NCH]])).then_inc(sV.h, 1))
        vCoef = sV.inc()
        ctx.wait("tensor", sV, vCoef)
        ctx.wait("tensor", sA, vExp)
        if "oevac" in ev:
            ctx.wait("tensor", sV, ev["oevac"])
        if "ofev" in ev:
            ctx.wait("tensor", sA, ev["ofev"])
        if "stev" in ev:
            ctx.wait("tensor", sV, ev["stev"])
        if "y0ev" in ev:
            ctx.wait("tensor", sA, ev["y0ev"])
        I8 = cst.i8()
        for c in range(NCH):
            for qi, sb_ in enumerate((17, 18, 10)):
                ctx.op("tensor", lambda e, c=c, qi=qi, sb=sb_: e.transpose(
                    b.pE[0:64, 192 + c * 24 + qi * 8:192 + c * 24 + qi * 8 + 8],
                    gw[0:8, sb * SEG + c * L:sb * SEG + (c + 1) * L], I8))
        for h in range(NH):
            ctx.op("tensor", lambda e, h=h: e.matmul(
                b.pBlb[0:64, h * NCH:(h + 1) * NCH],
                cst.onehot(h), b.bend[0:8, 0:NCH], start=True, stop=True))
        ctx.inc_last("tensor", sP)
        vT = sP.inc()
        ctx.wait("vector", sP, vT)
        ctx.op("vector", lambda e: e.tensor_copy(b.q3t[:, :], b.pE[0:64, 192:192 + NCH * 24]))
        ctx.op("vector", lambda e: e.tensor_copy(b.blb[:, :], b.pBlb[0:64, :]).then_inc(sV.h, 1))
        vQ3 = sV.inc()
        ev["q3t"] = vQ3
        ev["gates_done"] = vQ3

    def emit_prep(self, c, qkv_wait=None):
        cp = c % 2
        ctx, b, cst = self.ctx, self.b, self.c
        sP, sV, sA, sG = self.sP, self.sV, self.sA, self.sG
        ev = self.ev
        c0 = c * L

        if "rhat1" in ev:
            ctx.wait("tensor", sV, ev["rhat1"])
        if qkv_wait is not None:
            ctx.wait("tensor", qkv_wait[0], qkv_wait[1])
        for h in range(NH):
            ctx.op("tensor", lambda e, h=h: e.matmul(
                b.pF[0:64, h * 64:(h + 1) * 64], hs0(b.kf, h, c0, L),
                hs0(b.kf, h, c0, L), start=True, stop=True))
        ctx.inc_last("tensor", sP)
        ev["kkmm"] = sP.inc()
        ctx.wait("scalar", sP, ev["kkmm"])
        ctx.op("scalar", lambda e: e.copy(b.grsb[:, :], b.pF[0:64, :]).then_inc(sA.h, 1))
        ev["kkev"] = sA.inc()
        ctx.wait("tensor", sA, ev["kkev"])
        for h in range(NH):
            ctx.op("tensor", lambda e, h=h: e.matmul(
                b.pF[0:64, h * 64:(h + 1) * 64], hs0(b.kf, h, c0, L),
                hs0(b.qf, h, c0, L), start=True, stop=True))
        ctx.inc_last("tensor", sP)
        ev["kqmm"] = sP.inc()
        ctx.wait("scalar", sP, ev["kqmm"])
        ctx.op("scalar", lambda e: e.copy(b.kqraw[:, :], b.pF[0:64, :]).then_inc(sA.h, 1))
        ev["kqev"] = sA.inc()
        if "ktv" in ev:
            ctx.wait("tensor", sA, ev["ktv"])
            ctx.wait("tensor", sV, ev["vtv"])
        for h in range(NH):
            ctx.op("tensor", lambda e, h=h: e.transpose(
                b.pV[0:64, h * 64:(h + 1) * 64],
                hs0(b.vf, h, c0, L), cst.I64()))
        for h in range(NH):
            ctx.op("tensor", lambda e, h=h: e.transpose(
                b.pV[0:64, 512 + h * 64:512 + (h + 1) * 64],
                hs0(b.kf, h, c0, L), cst.I64()))
        ctx.inc_last("tensor", sP)
        ev["vkt"] = sP.inc()

        ctx.wait("gpsimd", sA, ev["kkev"])
        ctx.wait("gpsimd", sV, ev["gates_done"])
        smT = bcast_h(b.q3t, 0, 64, c * 24 + 8, 1, 8, 64)
        psiT = bcast_h(b.q3t, 0, 64, c * 24 + 0, 1, 8, 64)
        x3 = b.xraw[:, :].rearrange("p (h j) -> p h j", h=NH)
        g3 = b.grsb[:, :].rearrange("p (h j) -> p h j", h=NH)
        ctx.op("gpsimd", lambda e, smT=smT: e.tensor_tensor(x3, g3, smT, ALU.mult).then_inc(sG.h, 1))
        vXr = sG.inc()
        mLn = bass.AP(cst.bf.handle if hasattr(cst.bf, "handle") else cst.bf,
                      128, [[Consts.WIDTH, 64], [0, 8], [1, 64]])
        xa3 = b.xall[:, :].rearrange("p (h j) -> p h j", h=NH)
        ctx.wait("gpsimd", sG, vXr)
        ctx.op("gpsimd", lambda e: e.tensor_tensor(xa3, x3, mLn, ALU.mult).then_inc(sG.h, 1))
        ev["xall"] = sG.inc()
        ctx.wait("gpsimd", sA, ev["kqev"])
        mU = bass.AP(cst.bf.handle if hasattr(cst.bf, "handle") else cst.bf,
                     192, [[Consts.WIDTH, 64], [0, 8], [1, 64]])
        kq3 = b.kqgm[cp][:, :].rearrange("p (h j) -> p h j", h=NH)
        kqr3 = b.kqraw[:, :].rearrange("p (h j) -> p h j", h=NH)
        ctx.op("gpsimd", lambda e: e.tensor_tensor(kq3, kqr3, mU, ALU.mult).then_inc(sG.h, 1))
        ev["kqgm"] = sG.inc()

        ctx.wait("tensor", sG, ev["xall"])
        if "stev" in ev:
            ctx.wait("tensor", sV, ev["stev"])
        if "y0ev" in ev:
            ctx.wait("tensor", sA, ev["y0ev"])
        for p in range(4):
            ctx.op("tensor", lambda e, p=p: e.transpose(
                b.pDx[:, p * 64:(p + 1) * 64],
                b.xall[:, p * 128:(p + 1) * 128], cst.I64()))
        ctx.inc_last("tensor", sP)
        ev["xtr"] = sP.inc()
        ctx.wait("scalar", sP, ev["xtr"])
        y0h = b.y0.handle if hasattr(b.y0, "handle") else b.y0
        for par in range(2):
            out_ap = bass.AP(y0h, par * 64, [[NH * 64, 64], [128, 4], [1, 64]])
            ctx.op("scalar", lambda e, par=par, out_ap=out_ap: e.copy(
                out_ap, b.pDx[par * 64:par * 64 + 64, :]))
        ctx.inc_last("scalar", sA)
        ev["y0ev"] = sA.inc()

        ctx.wait("vector", sP, ev["vkt"])
        ctx.wait("vector", sV, ev["gates_done"])
        vt3 = b.vt[:, :].rearrange("p (h v) -> p h v", h=NH)
        pv3 = b.pV[0:64, 0:512].rearrange("p (h v) -> p h v", h=NH)
        ctx.op("vector", lambda e, psiT=psiT: e.tensor_tensor(vt3, pv3, psiT, ALU.mult).then_inc(sV.h, 1))
        ev["vtv"] = sV.inc()
        ctx.wait("scalar", sP, ev["vkt"])
        ctx.op("scalar", lambda e, cp=cp: e.copy(b.kt[cp][:, :], b.pV[0:64, 512:1024]).then_inc(sA.h, 1))
        ev["ktv"] = sA.inc()

    def emit_chain(self, c, cur, prep_next=None):
        ctx, b, cst = self.ctx, self.b, self.c
        sP, sV, sA, sG = self.sP, self.sV, self.sA, self.sG
        ev = self.ev
        cp = c % 2
        c0 = c * L
        st0 = b.st[cur]
        st1 = b.st[1 - cur]

        ctx.wait("tensor", sA, ev["kqev"])
        if "stev" in ev:
            ctx.wait("tensor", sV, ev["stev"])
        for h in range(NH):
            ctx.op("tensor", lambda e, h=h: e.matmul(
                b.pF[0:64, h * 64:(h + 1) * 64], hs0(b.kf, h, c0, L),
                st0[0:64, h * 64:(h + 1) * 64], start=True, stop=True))
        ctx.inc_last("tensor", sP)
        ev["ks0"] = sP.inc()

        ctx.wait("vector", sP, ev["ks0"])
        ctx.wait("vector", sV, ev["vtv"])
        smT = bcast_h(b.q3t, 0, 64, c * 24 + 8, 1, 8, 64)
        r3 = b.rsc[:, :].rearrange("p (h v) -> p h v", h=NH)
        pf3 = b.pF[0:64, :].rearrange("p (h v) -> p h v", h=NH)
        ctx.op("vector", lambda e, smT=smT: e.tensor_tensor(r3, pf3, smT, ALU.mult).then_inc(sV.h, 1))
        ev["rhat1"] = sV.inc()
        ctx.wait("vector", sV, ev["rhat1"])
        ctx.op("vector", lambda e: e.tensor_tensor(b.ubuf[0][:, :], b.vt[:, :], b.rsc[:, :], ALU.subtract).then_inc(sV.h, 1))
        ev["u0"] = sV.inc()

        ysrc = [b.y0, None, None, None, None, None]
        ucur = 0
        uev = ev["u0"]
        yev = [ev["y0ev"], None, None, None, None, None]
        yevsem = [sA, None, None, None, None, None]
        zsrc = b.xall
        zev, zevsem = ev["xall"], sG
        for k in range(6):
            ctx.wait("tensor", yevsem[k], yev[k])
            ctx.wait("tensor", sV, uev)
            for h in range(NH):
                ctx.op("tensor", lambda e, h=h, ys=ysrc[k], u=b.ubuf[ucur]: e.matmul(
                    b.pU[0:64, h * 64:(h + 1) * 64], ys[:, h * 64:(h + 1) * 64],
                    u[:, h * 64:(h + 1) * 64], start=True, stop=True))
            ctx.inc_last("tensor", sP)
            apv = sP.inc()
            if k < 5:
                ynew = b.ybuf[k % 2]
                znew = b.zbuf[k % 2]
                if "yev_d" in ev:
                    ctx.wait("tensor", sV, ev["yev_d"])
                if "zev_d" in ev:
                    ctx.wait("tensor", sA, ev["zev_d"])
                ctx.wait("tensor", yevsem[k], yev[k])
                ctx.wait("tensor", zevsem, zev)
                for h in range(NH):
                    ctx.op("tensor", lambda e, h=h, ys=ysrc[k], zs=zsrc: e.matmul(
                        b.pA[0:64, h * 64:(h + 1) * 64], zs[:, h * 64:(h + 1) * 64],
                        ys[:, h * 64:(h + 1) * 64], start=True, stop=True))
                if k < 4:
                    for h in range(NH):
                        ctx.op("tensor", lambda e, h=h, ys=ysrc[k], zs=zsrc: e.matmul(
                            b.pB[0:64, h * 64:(h + 1) * 64], ys[:, h * 64:(h + 1) * 64],
                            zs[:, h * 64:(h + 1) * 64], start=True, stop=True))
                ctx.inc_last("tensor", sP)
                sqv = sP.inc()
                ctx.wait("vector", sP, sqv)
                ctx.op("vector", lambda e, ynew=ynew: e.tensor_copy(ynew[:, :], b.pA[0:64, :]).then_inc(sV.h, 1))
                ev["yev_d"] = sV.inc()
                ysrc[k + 1] = ynew
                yev[k + 1] = ev["yev_d"]
                yevsem[k + 1] = sV
                if k < 4:
                    ctx.wait("scalar", sP, sqv)
                    ctx.op("scalar", lambda e, znew=znew: e.copy(znew[:, :], b.pB[0:64, :]).then_inc(sA.h, 1))
                    ev["zev_d"] = sA.inc()
                    zsrc = znew
                    zev, zevsem = ev["zev_d"], sA
            ctx.wait("vector", sP, apv)
            unext = 1 - ucur
            ctx.op("vector", lambda e, ui=ucur, uo=unext: e.tensor_tensor(
                b.ubuf[uo][:, :], b.pU[0:64, :], b.ubuf[ui][:, :], ALU.add).then_inc(sV.h, 1))
            uev = sV.inc()
            ucur = unext
            if prep_next is not None and k == 2:
                prep_next()
                prep_next = None
        ev["ufin"] = uev
        ufin = b.ubuf[ucur]

        ctx.wait("tensor", sV, uev)
        ctx.wait("tensor", sG, ev["kqgm"])
        ctx.wait("tensor", sV, ev["oevac"] if "oevac" in ev else ev["q3t"])
        for h in range(NH):
            ctx.op("tensor", lambda e, h=h: e.matmul(
                b.pE[0:64, h * 64:(h + 1) * 64], hs0(b.qf, h, c0, L),
                st0[0:64, h * 64:(h + 1) * 64], start=True, stop=False))
            ctx.op("tensor", lambda e, h=h, u=ufin, cp=cp: e.matmul(
                b.pE[0:64, h * 64:(h + 1) * 64], b.kqgm[cp][:, h * 64:(h + 1) * 64],
                u[:, h * 64:(h + 1) * 64], start=False, stop=True))
        ctx.inc_last("tensor", sP)
        ev["numm"] = sP.inc()
        ev["qs0"] = ev["numm"]
        ctx.wait("vector", sP, ev["numm"])
        bTc = bcast_h(b.q3t, 0, 64, c * 24 + 16, 1, 8, 64)
        o3 = b.osb[:, :].rearrange("p (h v) -> p h v", h=NH)
        pe3 = b.pE[0:64, :].rearrange("p (h v) -> p h v", h=NH)
        ctx.op("vector", lambda e, bTc=bTc: e.tensor_tensor(o3, pe3, bTc, ALU.mult).then_inc(sV.h, 1))
        ev["oevac"] = sV.inc()

        ctx.wait("tensor", sV, uev)
        ctx.wait("tensor", sA, ev["ktv"])
        if "ofev" in ev:
            ctx.wait("tensor", sA, ev["ofev"])
        if "y0ev" in ev:
            ctx.wait("tensor", sA, ev["y0ev"])
        if "stev" in ev:
            ctx.wait("tensor", sV, ev["stev"])
        for h in range(NH):
            ctx.op("tensor", lambda e, h=h, st0=st0: e.matmul(
                b.pS[0:64, h * 64:(h + 1) * 64], cst.I64(),
                st0[0:64, h * 64:(h + 1) * 64], start=True, stop=False))
            ctx.op("tensor", lambda e, h=h, u=ufin, cp=cp: e.matmul(
                b.pS[0:64, h * 64:(h + 1) * 64], b.kt[cp][:, h * 64:(h + 1) * 64],
                u[:, h * 64:(h + 1) * 64], start=False, stop=True))
        ctx.inc_last("tensor", sP)
        ev["smm"] = sP.inc()
        ctx.wait("vector", sP, ev["smm"])
        blbh = b.blb.handle if hasattr(b.blb, "handle") else b.blb
        blb_ap = bass.AP(blbh, c, [[NH * NCH, 64], [NCH, 8], [0, 64]])
        s13 = st1[:, :].rearrange("p (h v) -> p h v", h=NH)
        ps3 = b.pS[0:64, :].rearrange("p (h v) -> p h v", h=NH)
        ctx.op("vector", lambda e, blb_ap=blb_ap, s13=s13, ps3=ps3: e.tensor_tensor(
            s13, ps3, blb_ap, ALU.mult).then_inc(sV.h, 1))
        ev["stev"] = sV.inc()

        ctx.wait("tensor", sV, ev["oevac"])
        if "ofev" in ev:
            ctx.wait("tensor", sA, ev["ofev"])
        for p in range(4):
            ctx.op("tensor", lambda e, p=p: e.transpose(
                b.pDo[:, p * 64:(p + 1) * 64],
                b.osb[:, p * 128:(p + 1) * 128], cst.I64()))
        ctx.inc_last("tensor", sP)
        ev["otr"] = sP.inc()
        ctx.wait("scalar", sP, ev["otr"])
        for p in range(4):
            ctx.op("scalar", lambda e, p=p: e.copy(
                b.of[p][:, c0:c0 + L], b.pDo[:, p * 64:(p + 1) * 64]))
        ctx.inc_last("scalar", sA)
        ev["ofev"] = sA.inc()



class AttnBufs:
    def __init__(self, nc):
        sb = nc.alloc_sbuf_tensor
        self.xseg = [sb("a_xseg%d" % i, [128, 8 * SEG], BF16) for i in range(2)]
        self.x2 = sb("a_x2", [128, 8 * SEG], BF16)
        self.msb = sb("a_msb", [128, SEG], F32)
        self.vsb = sb("a_vsb", [128, SEG], F32)
        self.rsb = sb("a_rsb", [128, SEG], F32)
        self.wq = sb("a_wq", [128, 8 * 512], BF16)
        self.wk = sb("a_wk", [128, 8 * 512], BF16)
        self.wv = sb("a_wv", [128, 8 * 512], BF16)
        self.wgb = sb("a_wgb", [128, 8 * 16], BF16)
        self.cw = sb("a_cw", [128, 48], F32)
        self.cb = sb("a_cb", [128, 12], F32)
        self.zq = sb("a_zq", [128, 4 * (SEG + 3)], F32)
        self.zk = sb("a_zk", [128, 4 * (SEG + 3)], F32)
        self.zv = sb("a_zv", [128, 4 * (SEG + 3)], F32)
        self.cacc = sb("a_cacc", [128, 4 * SEG], F32)
        self.cout = sb("a_cout", [128, 4 * SEG], BF16)
        self.lnc = sb("a_lnc", [128, 2], F32)
        self.xng = sb("a_xng", [128, 8 * SEG], BF16)


class AttnEmitter:
    def __init__(self, ctx, ab, db, cst, em, xnd=None, nsegs=16):
        self.ctx = ctx
        self.ab = ab
        self.db = db
        self.cst = cst
        self.em = em
        self.xnd = xnd
        self.nsegs = nsegs
        s = ctx.sems
        self.sP = s["dP"]; self.sV = s["dV"]; self.sA = s["dA"]; self.sG = s["dG"]
        self.sD = s["lds"]

    def emit_segment(self, s, xbuf_idx, x_wait, w_wait=None):
        ctx, ab, db, cst = self.ctx, self.ab, self.db, self.cst
        em = self.em
        sP, sV, sA, sG, sD = self.sP, self.sV, self.sA, self.sG, self.sD
        xs = ab.xseg[xbuf_idx]
        ev = em.ev

        if s == 0:
            if w_wait is not None:
                ctx.wait("vector", w_wait[0], w_wait[1])
            ctx.op("vector", lambda e: e.memset(ab.lnc[:, 0:1], 1e-5))
            for z in (ab.zq, ab.zk, ab.zv):
                for pair in range(4):
                    ctx.op("vector", lambda e, z=z, pair=pair: e.memset(
                        z[:, pair * (SEG + 3):pair * (SEG + 3) + 3], 0.0))
            ctx.op("vector", lambda e: e.tensor_scalar_mul(ab.cw[:, 0:16], ab.cw[:, 0:16], 0.125))
            ctx.op("vector", lambda e: e.tensor_scalar_mul(ab.cb[:, 0:4], ab.cb[:, 0:4], 0.125))
            ctx.inc_last("vector", sV)
            sV.inc()

        ctx.wait("scalar", x_wait[0], x_wait[1])
        if "s2mm" in ev:
            ctx.wait("scalar", sP, ev["s2mm"])
        ctx.op("scalar", lambda e: e.activation(ab.x2[:, :], xs[:, :], AF.Square).then_inc(sA.h, 1))
        vX2 = sA.inc()
        ctx.wait("tensor", x_wait[0], x_wait[1])
        ctx.wait("tensor", sV, ev.get("oevac", 0))
        ctx.wait("tensor", sV, ev.get("rhat1", 0))
        for ci in range(8):
            ctx.op("tensor", lambda e, ci=ci: e.matmul(
                db.pE[:, 0:SEG], cst.lnones(),
                xs[:, ci * SEG:(ci + 1) * SEG], start=(ci == 0), stop=(ci == 7)))
        ctx.inc_last("tensor", sP)
        vSum = sP.inc()
        ctx.wait("tensor", sA, vX2)
        for ci in range(8):
            ctx.op("tensor", lambda e, ci=ci: e.matmul(
                db.pF[:, 0:SEG], cst.lnones(),
                ab.x2[:, ci * SEG:(ci + 1) * SEG], start=(ci == 0), stop=(ci == 7)))
        ctx.inc_last("tensor", sP)
        vSum2 = sP.inc()
        ev["s2mm"] = vSum2
        ctx.wait("vector", sP, vSum)
        ctx.op("vector", lambda e: e.tensor_copy(ab.msb[:, :], db.pE[:, 0:SEG]).then_inc(sV.h, 1))
        vM = sV.inc()
        ctx.wait("scalar", sV, vM)
        ctx.op("scalar", lambda e: e.activation(ab.vsb[:, :], ab.msb[:, :], AF.Square).then_inc(sA.h, 1))
        vM2 = sA.inc()
        ctx.wait("vector", sP, vSum2)
        ctx.wait("vector", sA, vM2)
        ctx.op("vector", lambda e: e.tensor_tensor(ab.vsb[:, :], db.pF[:, 0:SEG], ab.vsb[:, :], ALU.subtract).then_inc(sV.h, 1))
        vVar = sV.inc()
        ctx.wait("scalar", sV, vVar)
        ctx.op("scalar", lambda e: e.activation(ab.rsb[:, :], ab.vsb[:, :], AF.Ln, bias=ab.lnc[:, 0:1]))
        ctx.op("scalar", lambda e: e.activation(ab.rsb[:, :], ab.rsb[:, :], AF.Exp, scale=-0.5).then_inc(sA.h, 1))
        vR = sA.inc()
        x3 = ap3(xs, 0, 128, [[SEG, 8], [1, SEG]])
        mb = ap3(ab.msb, 0, 128, [[0, 8], [1, SEG]])
        ctx.wait("vector", sA, vX2)
        ctx.wait("vector", sP, vSum2)
        ctx.op("vector", lambda e, x3=x3, mb=mb: e.tensor_tensor(x3, x3, mb, ALU.subtract).then_inc(sV.h, 1))
        vSub = sV.inc()

        projs = [("q", ab.wq, ab.zq), ("k", ab.wk, ab.zk), ("v", ab.wv, ab.zv)]
        pbanks = [db.pE, db.pF]
        vEvac = {0: vM, 1: vVar}
        for mi in range(12):
            name, w, z = projs[mi // 4]
            pair = mi % 4
            bank = pbanks[mi % 2]
            ctx.wait("tensor", sV, vEvac[mi % 2])
            if mi == 0:
                ctx.wait("tensor", sV, vSub)
            for ci in range(8):
                ctx.op("tensor", lambda e, w=w, pair=pair, ci=ci, bank=bank: e.matmul(
                    bank[:, 0:512], w[:, ci * 512 + pair * 128:ci * 512 + (pair + 1) * 128],
                    xs[:, ci * SEG:(ci + 1) * SEG], start=(ci == 0), stop=(ci == 7)))
            ctx.inc_last("tensor", sP)
            vMM = sP.inc()
            ctx.wait("vector", sP, vMM)
            if mi == 0:
                ctx.wait("vector", sA, vR)
            ctx.op("vector", lambda e, z=z, pair=pair, bank=bank: e.tensor_tensor(
                z[:, pair * (SEG + 3) + 3:(pair + 1) * (SEG + 3)],
                bank[:, 0:512], ab.rsb[:, :], ALU.mult).then_inc(sV.h, 1))
            vEvac[mi % 2] = sV.inc()
        ev["oevac"] = vEvac[0]
        ev["rhat1"] = vEvac[1]

        ctx.wait("tensor", sV, ev.get("stev", 0))
        ctx.wait("tensor", sV, ev.get("ufin", 0))
        for half, bank in ((0, db.pS), (1, db.pU)):
            for ci in range(8):
                ctx.op("tensor", lambda e, ci=ci, half=half, bank=bank: e.matmul(
                    bank[0:8, 0:SEG],
                    ab.wgb[:, ci * 16 + half * 8:ci * 16 + half * 8 + 8],
                    xs[:, ci * SEG:(ci + 1) * SEG], start=(ci == 0), stop=(ci == 7)))
            ctx.inc_last("tensor", sP)
            sP.inc()
        vGb = sP.v
        ev["xsfree%d" % xbuf_idx] = vGb
        gw = db.gw
        ctx.wait("vector", sP, vGb)
        ctx.wait("vector", sA, ev.get("zw", 0))
        ctx.op("vector", lambda e: e.tensor_tensor(
            gw[0:8, 0:SEG], db.pS[0:8, 0:SEG], ab.rsb[0:8, :], ALU.mult))
        ctx.op("vector", lambda e: e.tensor_tensor(
            gw[0:8, SEG:2 * SEG], db.pU[0:8, 0:SEG], ab.rsb[0:8, :], ALU.mult))
        ctx.inc_last("vector", sV)
        self.vZrg = sV.inc()

        for pi, (name, w, z) in enumerate(projs):
            prev = projs[pi - 1][0] if pi > 0 else "v"
            if ("qkv_" + prev) in ev:
                ctx.wait("vector", sD, ev["qkv_" + prev])
            for pair in range(4):
                zb = pair * (SEG + 3)
                ca = ab.cacc[:, pair * SEG:(pair + 1) * SEG]
                co = ab.cout[:, pair * SEG:(pair + 1) * SEG]
                wc = pi * 16 + pair * 4
                ctx.op("vector", lambda e, z=z, zb=zb, ca=ca, wc=wc, pi=pi, pair=pair: e.tensor_scalar(
                    ca, z[:, zb:zb + SEG], ab.cw[:, wc:wc + 1],
                    ab.cb[:, pi * 4 + pair:pi * 4 + pair + 1], ALU.mult, ALU.add))
                for j in range(1, 3):
                    ctx.op("vector", lambda e, z=z, zb=zb, ca=ca, wc=wc, j=j: e.scalar_tensor_tensor(
                        ca, z[:, zb + j:zb + j + SEG], ab.cw[:, wc + j:wc + j + 1],
                        ca, ALU.mult, ALU.add))
                ctx.op("vector", lambda e, z=z, zb=zb, ca=ca, co=co, wc=wc: e.scalar_tensor_tensor(
                    co, z[:, zb + 3:zb + 3 + SEG], ab.cw[:, wc + 3:wc + 3 + 1],
                    ca, ALU.mult, ALU.add))
                ctx.op("vector", lambda e, z=z, zb=zb: e.tensor_copy(
                    z[:, zb:zb + 3], z[:, zb + SEG:zb + SEG + 3]))
            ctx.inc_last("vector", sV)
            vCout = sV.inc()
            dst = {"q": db.qf, "k": db.kf, "v": db.vf}[name]
            ctx.wait("sync", sV, vCout)
            ctx.wait("sync", sP, ev.get("smm", 0))
            couth = ab.cout.handle if hasattr(ab.cout, "handle") else ab.cout
            dsth = dst.handle if hasattr(dst, "handle") else dst
            for par in range(2):
                in_ap = bass.AP(couth, par * 64 * (4 * SEG), [[4 * SEG, 64], [SEG, 4], [1, SEG]])
                out_ap = bass.AP(dsth, par * SEG, [[NH * SEG, 64], [2 * SEG, 4], [1, SEG]])
                ctx.op("sync", lambda e, in_ap=in_ap, out_ap=out_ap: e.dma_start(
                    out_ap, in_ap).then_inc(sD.h, 16))
                sD.inc(16)
            ev["qkv_" + name] = sD.v

        if self.xnd is not None:
            GS = self.nsegs * 64
            if "xngd" in ev:
                ctx.wait("vector", sD, ev["xngd"])
            o_ap = ap3(ab.xng, 0, 128, [[SEG, 8], [64, 8], [1, 64]])
            i_x = ap3(xs, 0, 128, [[SEG, 8], [1, 8], [8, 64]])
            i_r = ap3(ab.rsb, 0, 128, [[0, 8], [1, 8], [8, 64]])
            ctx.op("vector", lambda e, o_ap=o_ap, i_x=i_x, i_r=i_r: e.tensor_tensor(
                o_ap, i_x, i_r, ALU.mult).then_inc(sV.h, 1))
            vXng = sV.inc()
            ev["xsfree_v%d" % xbuf_idx] = vXng
            ctx.wait("sync", sV, vXng)
            xngh = ab.xng.handle if hasattr(ab.xng, "handle") else ab.xng
            for ci in range(8):
                d_ap = bass.AP(self.xnd, ci * 128 * 8 * GS + s * 64,
                               [[8 * GS, 128], [GS, 8], [1, 64]])
                s_ap = bass.AP(xngh, ci * 512, [[8 * SEG, 128], [64, 8], [1, 64]])
                ctx.op("sync", lambda e, d_ap=d_ap, s_ap=s_ap: e.dma_start(
                    d_ap, s_ap).then_inc(sD.h, 16))
                sD.inc(16)
            ev["xngd"] = sD.v



class FfnBufs:
    BASE0 = 16640
    OFF_S = BASE0 + 122880

    def __init__(self, nc, gs):
        self.gs = gs
        self._off = self.BASE0

        def at(name, shape, dtype):
            nb = shape[1] * (2 if dtype == BF16 else 4)
            self._off = (self._off + 255) // 256 * 256
            t = nc.alloc_sbuf_tensor_at(name, shape, dtype, offset=self._off)
            self._off += nb
            return t

        self.w1c = [at("f_w1c%d" % i, [128, 8 * 1024], BF16) for i in range(2)]
        self.w2c = [at("f_w2c%d" % i, [128, 8 * 1024], BF16) for i in range(2)]
        self.xngc = at("f_xngc", [128, 8 * gs], BF16)
        self.hgt = at("f_hgt", [128, 8 * 512], BF16)
        self.yst = at("f_yst", [128, 8 * gs], BF16)
        self.b1sb = at("f_b1sb", [128, 64], F32)
        self.b2q = at("f_b2q", [128, 64], F32)
        assert self._off <= self.OFF_S, self._off
        self._off = self.OFF_S
        self.sw1l = at("f_sw1l", [128, 8 * 1024], BF16)
        self.sw3l = at("f_sw3l", [128, 8 * 1024], BF16)
        self.sw2l = at("f_sw2l", [128, 8 * 1024], BF16)
        self.wol = at("f_wol", [128, 4 * 1024], BF16)
        self.sbc = at("f_sbc", [128, 32], F32)
        self.s_end = self._off
        assert self.s_end <= nc.sbuf_base, (self.s_end, nc.sbuf_base)
        self._off = self.BASE0
        self.xt = [at("f_xt%d" % i, [128, 4096], F32) for i in range(2)]
        self.xb2 = at("f_xb2", [128, 4096], BF16)
        self.x22 = at("f_x22", [128, 4096], BF16)
        self.xn2 = at("f_xn2", [128, 4096], BF16)
        self.hsb = at("f_hsb", [128, 8 * 512], BF16)
        self.t1 = at("f_t1", [128, 512], BF16)
        self.t3 = at("f_t3", [128, 512], BF16)
        self.osb2 = at("f_osb2", [128, 4 * 512], BF16)
        self.ysb2 = at("f_ysb2", [128, 4096], BF16)
        self.part = [at("f_part%d" % i, [128, 4096], F32) for i in range(2)]
        self.m2 = at("f_m2", [128, 512], F32)
        self.v2 = at("f_v2", [128, 512], F32)
        self.r2 = at("f_r2", [128, 512], F32)
        assert self._off <= self.OFF_S, self._off


class FfnEmitter:
    def __init__(self, ctx, fb, db, cst, ev, nsegs):
        self.ctx = ctx
        self.fb = fb
        self.db = db
        self.cst = cst
        self.ev = ev
        self.nsegs = nsegs
        s = ctx.sems
        self.sP = s["dP"]; self.sV = s["dV"]; self.sA = s["dA"]; self.sG = s["dG"]
        self.sD = s["lds"]; self.sL = s["ldg"]

    def emit_barrier(self):
        ctx = self.ctx
        for eng in ENGINES:
            for sem in ctx.sems.values():
                if sem.v > 0:
                    ctx.wait(eng, sem, sem.v)

    def emit_c_prefetch(self, sw1b, sw3b, sw2b, wob, sb1in, sb3in, sb2in,
                        b1in, b2in):
        ctx, fb = self.ctx, self.fb
        sD, sV = self.sD, self.sV
        ctx.op("sync", lambda e: e.dma_start(fb.sw1l[:, :], sw1b).then_inc(sD.h, 16))
        ctx.op("sync", lambda e: e.dma_start(fb.sw3l[:, :], sw3b).then_inc(sD.h, 16))
        ctx.op("sync", lambda e: e.dma_start(fb.sw2l[:, :], sw2b).then_inc(sD.h, 16))
        ctx.op("sync", lambda e: e.dma_start(fb.wol[:, :], wob).then_inc(sD.h, 16))
        ctx.op("sync", lambda e: e.dma_start(fb.sbc[:, 0:8], sb1in).then_inc(sD.h, 16))
        ctx.op("sync", lambda e: e.dma_start(fb.sbc[:, 8:16], sb3in).then_inc(sD.h, 16))
        ctx.op("sync", lambda e: e.dma_start(fb.sbc[:, 16:24], sb2in).then_inc(sD.h, 16))
        ctx.op("sync", lambda e: e.dma_start(fb.b1sb[:, :], b1in).then_inc(sD.h, 16))
        ctx.op("sync", lambda e: e.dma_start(fb.b2q[:, :], b2in).then_inc(sD.h, 16))
        vLd = sD.inc(9 * 16)
        ctx.wait("vector", sD, vLd)
        ctx.op("vector", lambda e: e.tensor_scalar_mul(fb.sw2l[:, :], fb.sw2l[:, :], 0.5))
        ctx.op("vector", lambda e: e.tensor_scalar_mul(fb.sbc[:, 16:24], fb.sbc[:, 16:24], 0.25))
        ctx.op("vector", lambda e: e.tensor_scalar_mul(fb.b2q[:, :], fb.b2q[:, :], 0.25))
        ctx.op("vector", lambda e: e.memset(fb.sbc[:, 24:25], 1e-5))
        ctx.op("vector", lambda e: e.memset(fb.sbc[:, 25:26], 0.5).then_inc(self.sV.h, 1))
        self.ev["c2w"] = self.sV.inc()

    def emit_moe_group(self, gi, r, xnd, ydram, w1b, w2b):
        ctx, fb, db = self.ctx, self.fb, self.db
        sP, sV, sA, sD, sL = self.sP, self.sV, self.sA, self.sD, self.sL
        ev = self.ev
        GS = fb.gs
        NSUB = max(1, GS // 512)
        SUBN = min(GS, 512)
        wb = gi % 2

        if ("moe_wmm%d" % wb) in ev:
            ctx.wait("gpsimd", sP, ev["moe_wmm%d" % wb])
        ctx.op("gpsimd", lambda e, r=r, wb=wb: e.dma_start(
            fb.w1c[wb][:, :], w1b[:, r * 8192:(r + 1) * 8192]).then_inc(sL.h, 16))
        ctx.op("gpsimd", lambda e, r=r, wb=wb: e.dma_start(
            fb.w2c[wb][:, :], w2b[:, r * 8192:(r + 1) * 8192]).then_inc(sL.h, 16))
        vW = sL.inc(32)
        if "moe_xmm" in ev:
            ctx.wait("sync", sP, ev["moe_xmm"])
        xnd_ap = bass.AP(xnd, r * GS, [[8 * GS, 128], [128 * 8 * GS, 8], [1, GS]])
        ctx.op("sync", lambda e, xnd_ap=xnd_ap: e.dma_start(
            fb.xngc[:, :], xnd_ap).then_inc(sD.h, 16))
        vX = sD.inc(16)

        hbank = [db.pE, db.pF]
        ybank = [db.pU, db.pS]
        for sub in range(NSUB):
            for mh in range(8):
                bank = hbank[mh % 2]
                if mh < 2:
                    ctx.wait("tensor", sL, vW)
                    ctx.wait("tensor", sD, vX)
                k = "moe_hev%d" % (mh % 2)
                if k in ev:
                    ctx.wait("tensor", sA, ev[k])
                for ci in range(8):
                    ctx.op("tensor", lambda e, mh=mh, ci=ci, sub=sub, bank=bank, wb=wb: e.matmul(
                        bank[:, 0:SUBN], fb.w1c[wb][:, ci * 1024 + mh * 128:ci * 1024 + (mh + 1) * 128],
                        fb.xngc[:, ci * GS + sub * 512:ci * GS + sub * 512 + SUBN],
                        start=(ci == 0), stop=(ci == 7)))
                ctx.inc_last("tensor", sP)
                vH = sP.inc()
                ctx.wait("scalar", sP, vH)
                if mh == 0 and "moe_ymm" in ev:
                    ctx.wait("scalar", sP, ev["moe_ymm"])
                ctx.op("scalar", lambda e, mh=mh, r=r, bank=bank: e.activation(
                    fb.hgt[:, mh * 512:mh * 512 + SUBN], bank[:, 0:SUBN],
                    AF.Gelu, bias=fb.b1sb[:, r * 8 + mh:r * 8 + mh + 1]).then_inc(sA.h, 1))
                ev["moe_hev%d" % (mh % 2)] = sA.inc()
            vHl = ev["moe_hev1"]
            for mo in range(8):
                bank = ybank[mo % 2]
                if mo < 2:
                    ctx.wait("tensor", sA, vHl)
                k = "moe_yev%d" % (mo % 2)
                if k in ev:
                    ctx.wait("tensor", sV, ev[k])
                for hi in range(8):
                    ctx.op("tensor", lambda e, mo=mo, hi=hi, bank=bank, wb=wb: e.matmul(
                        bank[:, 0:SUBN], fb.w2c[wb][:, hi * 1024 + mo * 128:hi * 1024 + (mo + 1) * 128],
                        fb.hgt[:, hi * 512:hi * 512 + SUBN],
                        start=(hi == 0), stop=(hi == 7)))
                ctx.inc_last("tensor", sP)
                vY = sP.inc()
                ev["moe_ymm"] = vY
                ev["moe_wmm%d" % wb] = vY
                ev["moe_xmm"] = vY
                ctx.wait("vector", sP, vY)
                if mo == 0 and sub == 0 and "moe_ydma" in ev:
                    ctx.wait("vector", sD, ev["moe_ydma"])
                ctx.op("vector", lambda e, mo=mo, sub=sub, r=r, bank=bank: e.tensor_scalar(
                    fb.yst[:, mo * GS + sub * 512:mo * GS + sub * 512 + SUBN],
                    bank[:, 0:SUBN], 0.25,
                    fb.b2q[:, r * 8 + mo:r * 8 + mo + 1],
                    ALU.mult, ALU.add).then_inc(sV.h, 1))
                ev["moe_yev%d" % (mo % 2)] = sV.inc()
        ctx.wait("sync", sV, sV.v)
        y_ap = bass.AP(ydram, r * GS, [[8 * GS, 128], [128 * 8 * GS, 8], [1, GS]])
        ctx.op("sync", lambda e, y_ap=y_ap: e.dma_start(
            y_ap, fb.yst[:, :]).then_inc(sD.h, 16))
        ev["moe_ydma"] = sD.inc(16)

    def emit_c2_tile(self, ti, xin_t, odram, ydram, partd):
        ctx, fb, db, cst = self.ctx, self.fb, self.db, self.cst
        sP, sV, sA, sD, sL = self.sP, self.sV, self.sA, self.sD, self.sL
        ev = self.ev
        Tl = self.nsegs * SEG
        GS = fb.gs
        tb = ti % 2
        xt = fb.xt[tb]

        for k in ("c2_xr_v%d" % tb, "c2_xr_a%d" % tb):
            if k in ev:
                ctx.wait("gpsimd", sV if k.startswith("c2_xr_v") else sA, ev[k])
        x_ap = bass.AP(xin_t, ti * SEG, [[Tl, 128], [128 * Tl, 8], [1, SEG]])
        ctx.op("gpsimd", lambda e, x_ap=x_ap, xt=xt: e.dma_start(
            xt[:, :], x_ap).then_inc(sL.h, 16))
        vXl = sL.inc(16)
        if "c2_omm" in ev:
            ctx.wait("sync", sP, ev["c2_omm"])
        o_ap = bass.AP(odram, ti * SEG, [[Tl, 128], [128 * Tl, 4], [1, SEG]])
        ctx.op("sync", lambda e, o_ap=o_ap: e.dma_start(
            fb.osb2[:, :], o_ap).then_inc(sD.h, 16))
        vOl = sD.inc(16)
        if "c2_yfree" in ev:
            ctx.wait("sync", sV, ev["c2_yfree"])
        ysbh = fb.ysb2.handle if hasattr(fb.ysb2, "handle") else fb.ysb2
        for ci in range(8):
            d_ap = bass.AP(ysbh, ci * 512, [[4096, 128], [64, 8], [1, 64]])
            s_ap = bass.AP(ydram, ci * 128 * 8 * GS + ti * 64,
                           [[8 * GS, 128], [GS, 8], [1, 64]])
            ctx.op("sync", lambda e, d_ap=d_ap, s_ap=s_ap: e.dma_start(
                d_ap, s_ap).then_inc(sD.h, 16))
        vYl = sD.inc(8 * 16)

        ctx.wait("vector", sL, vXl)
        if "c2_xbmm" in ev:
            ctx.wait("vector", sP, ev["c2_xbmm"])
        ctx.op("vector", lambda e, xt=xt: e.tensor_copy(fb.xb2[:, :], xt[:, :]).then_inc(sV.h, 1))
        vXb = sV.inc()
        ctx.wait("scalar", sL, vXl)
        if "c2_s2mm" in ev:
            ctx.wait("scalar", sP, ev["c2_s2mm"])
        ctx.op("scalar", lambda e, xt=xt: e.activation(fb.x22[:, :], xt[:, :], AF.Square).then_inc(sA.h, 1))
        vX2 = sA.inc()
        ev["c2_xr_a%d" % tb] = vX2
        ctx.wait("tensor", sV, vXb)
        for k in ("c2_hev0", "c2_hev1"):
            if k in ev:
                ctx.wait("tensor", sA, ev[k])
        for ci in range(8):
            ctx.op("tensor", lambda e, ci=ci: e.matmul(
                db.pE[:, 0:SEG], cst.lnones(),
                fb.xb2[:, ci * SEG:(ci + 1) * SEG], start=(ci == 0), stop=(ci == 7)))
        ctx.inc_last("tensor", sP)
        vSum = sP.inc()
        ctx.wait("tensor", sA, vX2)
        for ci in range(8):
            ctx.op("tensor", lambda e, ci=ci: e.matmul(
                db.pF[:, 0:SEG], cst.lnones(),
                fb.x22[:, ci * SEG:(ci + 1) * SEG], start=(ci == 0), stop=(ci == 7)))
        ctx.inc_last("tensor", sP)
        vSum2 = sP.inc()
        ev["c2_s2mm"] = vSum2
        ctx.wait("vector", sP, vSum)
        ctx.op("vector", lambda e: e.tensor_copy(fb.m2[:, :], db.pE[:, 0:SEG]).then_inc(sV.h, 1))
        vM = sV.inc()
        ctx.wait("scalar", sV, vM)
        ctx.op("scalar", lambda e: e.activation(fb.v2[:, :], fb.m2[:, :], AF.Square).then_inc(sA.h, 1))
        vM2 = sA.inc()
        ctx.wait("vector", sP, vSum2)
        ctx.wait("vector", sA, vM2)
        ctx.op("vector", lambda e: e.tensor_tensor(fb.v2[:, :], db.pF[:, 0:SEG], fb.v2[:, :], ALU.subtract).then_inc(sV.h, 1))
        vVar = sV.inc()
        ctx.wait("scalar", sV, vVar)
        ctx.op("scalar", lambda e: e.activation(fb.r2[:, :], fb.v2[:, :], AF.Ln, bias=fb.sbc[:, 24:25]))
        ctx.op("scalar", lambda e: e.activation(fb.r2[:, :], fb.r2[:, :], AF.Exp, scale=-0.5).then_inc(sA.h, 1))
        vR = sA.inc()
        x3 = ap3(fb.xb2, 0, 128, [[SEG, 8], [1, SEG]])
        n3 = ap3(fb.xn2, 0, 128, [[SEG, 8], [1, SEG]])
        mb = ap3(fb.m2, 0, 128, [[0, 8], [1, SEG]])
        rb = ap3(fb.r2, 0, 128, [[0, 8], [1, SEG]])
        ctx.wait("vector", sA, vX2)
        ctx.wait("vector", sP, vSum2)
        ctx.op("vector", lambda e, x3=x3, mb=mb: e.tensor_tensor(x3, x3, mb, ALU.subtract))
        if "c2_nmm" in ev:
            ctx.wait("vector", sP, ev["c2_nmm"])
        ctx.wait("vector", sA, vR)
        ctx.op("vector", lambda e, x3=x3, n3=n3, rb=rb: e.tensor_tensor(n3, x3, rb, ALU.mult).then_inc(sV.h, 1))
        vXn = sV.inc()
        ev["c2_xbmm"] = vSum
        ev["c2_xr_v%d" % tb] = vXn

        pair0 = (db.pE, db.pF)
        pair1 = (db.pA, db.pB)
        if "c2w" in ev:
            ctx.wait("tensor", sV, ev["c2w"])
            del ev["c2w"]
        for mh in range(8):
            bA, bB = pair0 if mh % 2 == 0 else pair1
            if mh == 0:
                ctx.wait("tensor", sV, vM)
                ctx.wait("tensor", sV, vVar)
                ctx.wait("tensor", sV, vXn)
            k = "c2_hev%d" % (mh % 2)
            if k in ev and mh >= 2:
                ctx.wait("tensor", sA, ev[k])
            for ci in range(8):
                ctx.op("tensor", lambda e, mh=mh, ci=ci, bA=bA: e.matmul(
                    bA[:, 0:SEG], fb.sw1l[:, ci * 1024 + mh * 128:ci * 1024 + (mh + 1) * 128],
                    fb.xn2[:, ci * SEG:(ci + 1) * SEG], start=(ci == 0), stop=(ci == 7)))
            for ci in range(8):
                ctx.op("tensor", lambda e, mh=mh, ci=ci, bB=bB: e.matmul(
                    bB[:, 0:SEG], fb.sw3l[:, ci * 1024 + mh * 128:ci * 1024 + (mh + 1) * 128],
                    fb.xn2[:, ci * SEG:(ci + 1) * SEG], start=(ci == 0), stop=(ci == 7)))
            ctx.inc_last("tensor", sP)
            vHH = sP.inc()
            ev["c2_nmm"] = vHH
            ctx.wait("scalar", sP, vHH)
            if mh > 0:
                ctx.wait("scalar", sV, ev["c2_t13"])
            ctx.op("scalar", lambda e, mh=mh, bA=bA: e.activation(
                fb.t1[:, :], bA[:, 0:SEG], AF.Silu, bias=fb.sbc[:, mh:mh + 1]))
            ctx.op("scalar", lambda e, mh=mh, bB=bB: e.activation(
                fb.t3[:, :], bB[:, 0:SEG], AF.Identity, bias=fb.sbc[:, 8 + mh:9 + mh]).then_inc(sA.h, 1))
            ev["c2_hev%d" % (mh % 2)] = sA.inc()
            ctx.wait("vector", sA, ev["c2_hev%d" % (mh % 2)])
            if mh == 0 and "c2_omm" in ev:
                ctx.wait("vector", sP, ev["c2_omm"])
            ctx.op("vector", lambda e, mh=mh: e.tensor_tensor(
                fb.hsb[:, mh * 512:(mh + 1) * 512], fb.t1[:, :], fb.t3[:, :], ALU.mult).then_inc(sV.h, 1))
            ev["c2_t13"] = sV.inc()
        vHsb = ev["c2_t13"]

        part = fb.part[tb]
        ybank = [db.pU, db.pS]
        for mo in range(8):
            bank = ybank[mo % 2]
            if mo < 2:
                ctx.wait("tensor", sV, vHsb)
                ctx.wait("tensor", sD, vOl)
            k = "c2_pev%d" % (mo % 2)
            if k in ev:
                ctx.wait("tensor", sV, ev[k])
            for ko in range(4):
                ctx.op("tensor", lambda e, mo=mo, ko=ko, bank=bank: e.matmul(
                    bank[:, 0:SEG], fb.wol[:, ko * 1024 + mo * 128:ko * 1024 + (mo + 1) * 128],
                    fb.osb2[:, ko * SEG:(ko + 1) * SEG], start=(ko == 0), stop=False))
            for hi in range(8):
                ctx.op("tensor", lambda e, mo=mo, hi=hi, bank=bank: e.matmul(
                    bank[:, 0:SEG], fb.sw2l[:, hi * 1024 + mo * 128:hi * 1024 + (mo + 1) * 128],
                    fb.hsb[:, hi * SEG:(hi + 1) * SEG], start=False, stop=(hi == 7)))
            ctx.inc_last("tensor", sP)
            vMO = sP.inc()
            ev["c2_omm"] = vMO
            ctx.wait("vector", sP, vMO)
            if mo == 0:
                ctx.wait("vector", sD, vYl)
                if ("c2_pdma%d" % tb) in ev:
                    ctx.wait("vector", sD, ev["c2_pdma%d" % tb])
            ctx.op("vector", lambda e, mo=mo, bank=bank, part=part, xt=xt: e.scalar_tensor_tensor(
                part[:, mo * 512:(mo + 1) * 512], xt[:, mo * 512:(mo + 1) * 512],
                fb.sbc[:, 25:26], bank[:, 0:SEG], ALU.mult, ALU.add))
            p_st = bass.AP(part.handle if hasattr(part, "handle") else part,
                           mo * 512, [[4096, 128], [1, 8], [8, 64]])
            y_st = bass.AP(ysbh, mo * 512, [[4096, 128], [64, 8], [1, 64]])
            ctx.op("vector", lambda e, p_st=p_st, y_st=y_st, mo=mo: e.scalar_tensor_tensor(
                p_st, y_st, fb.sbc[:, 16 + mo:17 + mo], p_st, ALU.add, ALU.add).then_inc(sV.h, 1))
            ev["c2_pev%d" % (mo % 2)] = sV.inc()
        ev["c2_yfree"] = sV.v
        ev["c2_xr_v%d" % tb] = sV.v
        ctx.wait("sync", sV, sV.v)
        p_ap = bass.AP(partd, ti * SEG, [[Tl, 128], [128 * Tl, 8], [1, SEG]])
        ctx.op("sync", lambda e, p_ap=p_ap, part=part: e.dma_start(
            p_ap, part[:, :]).then_inc(sD.h, 16))
        ev["c2_pdma%d" % tb] = sD.inc(16)

    def emit_finale(self, partd_t, recv_t, oout, rg):
        ctx = self.ctx
        sD, sL = self.sD, self.sL
        scc = ctx.sems["cc"]
        sso = ctx.sems["so"]
        ctx.wait("gpsimd", sD, sD.v)
        ctx.op("gpsimd", lambda e: e.collective_compute(
            "ReduceScatter", ALU.add, replica_groups=rg,
            ins=[partd_t.ap().opt()], outs=[recv_t.ap().opt()]).then_inc(scc.h, 1))
        scc.inc()
        ctx.wait("gpsimd", scc, scc.v)
        ctx.op("gpsimd", lambda e: e.dma_start(oout, recv_t.ap()).then_inc(sso.h, 16))
        v = sso.inc(16)
        ctx.wait("gpsimd", sso, v)



def build_program(nsegs=16, ncores=8, debug=False):
    Tl = nsegs * SEG
    GS = nsegs * 64
    nc = bass.Bass("TRN2", target_bir_lowering=False, detect_race_conditions=False)
    dt = nc.dram_tensor
    xin_t = dt("xin", [1024, Tl], F32, kind="ExternalInput")
    xin = xin_t.ap()
    ins = {}
    for nm, sh in (("wqin", [128, 4096]), ("wkin", [128, 4096]), ("wvin", [128, 4096]),
                   ("wgbin", [128, 128]), ("cwin", [128, 48]), ("cbin", [128, 12]),
                   ("cin", [128, Consts.WIDTH]), ("woin", [128, 4096]),
                   ("sw1in", [128, 8192]), ("sw3in", [128, 8192]), ("sw2in", [128, 8192]),
                   ("sb1in", [128, 8]), ("sb3in", [128, 8]), ("sb2in", [128, 8]),
                   ("w1in", [128, 65536]), ("w2in", [128, 65536]),
                   ("b1in", [128, 64]), ("b2in", [128, 64])):
        ins[nm] = dt(nm, sh, F32, kind="ExternalInput").ap()
    oout = dt("oout", [512, Tl], F32, kind="ExternalOutput").ap()
    xnd_t = dt("xnd", [1024, 8 * GS], BF16)
    odram_t = dt("odram", [512, Tl], BF16)
    ydram_t = dt("ydram", [1024, 8 * GS], BF16)
    w1b_t = dt("w1b", [128, 65536], BF16)
    w2b_t = dt("w2b", [128, 65536], BF16)
    sw1b_t = dt("sw1b", [128, 8192], BF16)
    sw3b_t = dt("sw3b", [128, 8192], BF16)
    sw2b_t = dt("sw2b", [128, 8192], BF16)
    wob_t = dt("wob", [128, 4096], BF16)
    partd_t = dt("partd", [1024, Tl], F32)
    recv_t = dt("recv", [512, Tl], F32)

    ctx = Ctx(nc)
    ctx.sems = {n: ctx.sem(n) for n in ("dP", "dV", "dA", "dG", "ldg", "lds", "so", "cc")}
    db = DeltaBufs(nc)
    cst = Consts(nc)
    ab = AttnBufs(nc)
    em = DeltaEmitter(ctx, db, cst)
    at = AttnEmitter(ctx, ab, db, cst, em, xnd=xnd_t, nsegs=nsegs)
    fb = FfnBufs(nc, GS)
    fe = FfnEmitter(ctx, fb, db, cst, em.ev, nsegs)
    sldg = ctx.sems["ldg"]
    slds = ctx.sems["lds"]
    sso = ctx.sems["so"]
    sP, sV, sA = ctx.sems["dP"], ctx.sems["dV"], ctx.sems["dA"]
    ev = em.ev

    ctx.op("gpsimd", lambda e: e.dma_start(cst.bf[:, :], ins["cin"]).then_inc(sldg.h, 16))
    ctx.op("gpsimd", lambda e: e.dma_start(ab.wq[:, :], ins["wqin"]).then_inc(sldg.h, 16))
    ctx.op("gpsimd", lambda e: e.dma_start(ab.wk[:, :], ins["wkin"]).then_inc(sldg.h, 16))
    ctx.op("gpsimd", lambda e: e.dma_start(ab.wv[:, :], ins["wvin"]).then_inc(sldg.h, 16))
    ctx.op("gpsimd", lambda e: e.dma_start(ab.wgb[:, :], ins["wgbin"]).then_inc(sldg.h, 16))
    sldg.inc(80)
    ctx.op("sync", lambda e: e.dma_start(cst.f24[:, :], ins["cin"][0:24, 768:792]).then_inc(slds.h, 16))
    ctx.op("sync", lambda e: e.dma_start(cst.oh32[:, :], ins["cin"][0:8, 256:768]).then_inc(slds.h, 16))
    ctx.op("sync", lambda e: e.dma_start(cst.i8d[:, :], ins["cin"][:, 856:864]).then_inc(slds.h, 16))
    ctx.op("sync", lambda e: e.dma_start(ab.cw[:, :], ins["cwin"]).then_inc(slds.h, 16))
    ctx.op("sync", lambda e: e.dma_start(ab.cb[:, :], ins["cbin"]).then_inc(slds.h, 16))
    v_wcb = slds.inc(80)
    ctx.op("vector", lambda e: e.memset(db.st[0][:, :], 0.0).then_inc(sV.h, 1))
    ev["stev"] = sV.inc()

    cast_jobs = []
    for srcn, dstt, w in (("w1in", w1b_t, 65536), ("w2in", w2b_t, 65536),
                          ("sw1in", sw1b_t, 8192), ("sw3in", sw3b_t, 8192),
                          ("sw2in", sw2b_t, 8192), ("woin", wob_t, 4096)):
        for c0 in range(0, w, 8192):
            cast_jobs.append((ins[srcn], dstt, c0, min(8192, w - c0)))
    cast_per_seg = (len(cast_jobs) + nsegs - 1) // max(1, nsegs - 2)

    ostore = 0
    for s in range(nsegs):
        bi = s % 2
        if ("xsfree%d" % bi) in ev:
            ctx.wait("gpsimd", sP, ev["xsfree%d" % bi])
        if ("xsfree_v%d" % bi) in ev:
            ctx.wait("gpsimd", sV, ev["xsfree_v%d" % bi])
        in_ap = bass.AP(xin_t, s * SEG, [[Tl, 128], [128 * Tl, 8], [1, SEG]])
        ctx.op("gpsimd", lambda e, bi=bi, in_ap=in_ap: e.dma_start(
            ab.xseg[bi][:, :], in_ap).then_inc(sldg.h, 16))
        vx = sldg.inc(16)
        if s >= 2:
            for _ in range(cast_per_seg):
                if cast_jobs:
                    srcw, dstt, c0, n = cast_jobs.pop(0)
                    ctx.op("gpsimd", lambda e, srcw=srcw, dstt=dstt, c0=c0, n=n: e.dma_start(
                        bass.AP(dstt, c0, [[dstt.shape[1], 128], [1, n]]),
                        srcw[:, c0:c0 + n]).then_inc(sldg.h, 16))
                    sldg.inc(16)
        at.emit_segment(s, bi, x_wait=(sldg, vx), w_wait=(slds, v_wcb))
        em.emit_gates(None, kf_wait=(slds, ev["qkv_k"]), z_wait=(sV, at.vZrg))
        em.emit_prep(0, qkv_wait=(slds, ev["qkv_v"]))
        cur = 0
        for c in range(NCH):
            pn = None
            if c + 1 < NCH:
                pn = (lambda c=c: em.emit_prep(c + 1))
            em.emit_chain(c, cur, prep_next=pn)
            cur = 1 - cur
        if "ofev" in ev:
            ctx.wait("gpsimd", sA, ev["ofev"])
        for p in range(4):
            ctx.op("gpsimd", lambda e, p=p, s=s: e.dma_start(
                odram_t.ap()[p * 128:(p + 1) * 128, s * SEG:(s + 1) * SEG],
                db.of[p][:, :]).then_inc(sso.h, 16))
        ostore = sso.inc(64)
        if s + 1 < nsegs:
            ctx.wait("scalar", sso, ostore)
    for srcw, dstt, c0, n in cast_jobs:
        ctx.op("gpsimd", lambda e, srcw=srcw, dstt=dstt, c0=c0, n=n: e.dma_start(
            bass.AP(dstt, c0, [[dstt.shape[1], 128], [1, n]]),
            srcw[:, c0:c0 + n]).then_inc(sldg.h, 16))
        sldg.inc(16)
    ctx.wait("gpsimd", sldg, sldg.v)
    ctx.wait("gpsimd", sso, ostore)

    fe.emit_barrier()
    fe.emit_c_prefetch(sw1b_t.ap(), sw3b_t.ap(), sw2b_t.ap(), wob_t.ap(),
                       ins["sb1in"], ins["sb3in"], ins["sb2in"],
                       ins["b1in"], ins["b2in"])
    for gi in range(8):
        fe.emit_moe_group(gi, gi, xnd_t, ydram_t, w1b_t.ap(), w2b_t.ap())
    fe.emit_barrier()
    for ti in range(nsegs):
        fe.emit_c2_tile(ti, xin_t, odram_t, ydram_t, partd_t)
    rg = [[2 * i, 2 * i + 1] for i in range(ncores // 2)]
    fe.emit_finale(partd_t, recv_t, oout, rg)
    if debug:
        dbg = {"dbg_xnd": xnd_t, "dbg_o": odram_t, "dbg_y": ydram_t,
               "dbg_part": partd_t}
        sso = ctx.sems["so"]
        for nm, t in dbg.items():
            dt_ = nc.dram_tensor(nm, list(t.shape), F32, kind="ExternalOutput").ap()
            ctx.op("gpsimd", lambda e, dt_=dt_, t=t: e.dma_start(dt_, t.ap()).then_inc(sso.h, 16))
            sso.inc(16)
        ctx.wait("gpsimd", sso, sso.v)

    with nc.Block() as block:
        ctx.play(block)
    return nc



def _w_layout(Wp):
    out = np.zeros((128, 4096), dtype=np.float32)
    for ci in range(8):
        for pair in range(4):
            for par in range(2):
                h = 2 * pair + par
                blk = Wp[h * 64:(h + 1) * 64, ci * 128:(ci + 1) * 128]
                c0 = ci * 512 + pair * 128 + par * 64
                out[:, c0:c0 + 64] = blk.T
    return out


def _cw_layout(cws):
    out = np.zeros((128, 48), dtype=np.float32)
    for pi in range(3):
        for pair in range(4):
            for par in range(2):
                h = 2 * pair + par
                out[par * 64:(par + 1) * 64, pi * 16 + pair * 4:pi * 16 + pair * 4 + 4] = \
                    cws[pi][h * 64:(h + 1) * 64, :]
    return out


def _cb_layout(cbs):
    out = np.zeros((128, 12), dtype=np.float32)
    for pi in range(3):
        for pair in range(4):
            for par in range(2):
                h = 2 * pair + par
                out[par * 64:(par + 1) * 64, pi * 4 + pair] = cbs[pi][h * 64:(h + 1) * 64]
    return out


_PROG_CACHE = {}


def _get_prog():
    if "nc" not in _PROG_CACHE:
        _PROG_CACHE["nc"] = build_program(16, 8)
    return _PROG_CACHE["nc"]


def _parse_inputs(inputs):
    g = lambda k: np.asarray(inputs[k], dtype=np.float32)
    return dict(
        x=g("L0_attn_x"), Wq=g("L0_attn_Wq"), Wk=g("L0_attn_Wk"),
        Wv=g("L0_attn_Wv"), Wo=g("L0_attn_Wo"), Wg=g("L0_attn_Wg"),
        Wb=g("L0_attn_Wb"),
        cw=[g("L0_attn_c%s_w" % n)[:, 0, :] for n in "qkv"],
        cb=[g("L0_attn_c%s_b" % n) for n in "qkv"],
        sw1=g("L0_ffn_sw1"), sb1=g("L0_ffn_sb1"), sw3=g("L0_ffn_sw3"),
        sb3=g("L0_ffn_sb3"), sw2=g("L0_ffn_sw2"), sb2=g("L0_ffn_sb2"),
        w1=g("L0_ffn_w1"), b1=g("L0_ffn_b1"), w2=g("L0_ffn_w2"),
        b2=g("L0_ffn_b2"))


def stage_core(p, xT, hg):
    Wq, Wk, Wv, Wo, Wg, Wb = p["Wq"], p["Wk"], p["Wv"], p["Wo"], p["Wg"], p["Wb"]
    sw1, sw3, sw2 = p["sw1"], p["sw3"], p["sw2"]
    sb1, sb3, sb2 = p["sb1"], p["sb3"], p["sb2"]
    w1, b1, w2, b2 = p["w1"], p["b1"], p["w2"], p["b2"]
    hs = slice(hg * 512, (hg + 1) * 512)
    wgb = np.zeros((128, 128), dtype=np.float32)
    for ci in range(8):
        wgb[:, ci * 16:ci * 16 + 8] = Wg[hg * 8:(hg + 1) * 8, ci * 128:(ci + 1) * 128].T
        wgb[:, ci * 16 + 8:ci * 16 + 16] = Wb[hg * 8:(hg + 1) * 8, ci * 128:(ci + 1) * 128].T
    woin = np.ascontiguousarray(
        np.transpose(Wo[:, hg * 512:(hg + 1) * 512].reshape(1024, 4, 128),
                     (2, 1, 0)).reshape(128, 4096))
    sw1l = np.ascontiguousarray(
        np.transpose(sw1[:, hg * 1024:(hg + 1) * 1024].reshape(8, 128, 1024),
                     (1, 0, 2)).reshape(128, 8192))
    sw3l = np.ascontiguousarray(
        np.transpose(sw3[:, hg * 1024:(hg + 1) * 1024].reshape(8, 128, 1024),
                     (1, 0, 2)).reshape(128, 8192))
    sw2l = np.ascontiguousarray(
        np.transpose(sw2[hg * 1024:(hg + 1) * 1024, :].reshape(8, 128, 1024),
                     (1, 0, 2)).reshape(128, 8192))
    sb1in = np.ascontiguousarray(sb1[hg * 1024:(hg + 1) * 1024].reshape(8, 128).T)
    sb3in = np.ascontiguousarray(sb3[hg * 1024:(hg + 1) * 1024].reshape(8, 128).T)
    sb2in = np.ascontiguousarray(sb2.reshape(8, 128).T)
    w1in = np.zeros((128, 65536), dtype=np.float32)
    w2in = np.zeros((128, 65536), dtype=np.float32)
    b1in = np.zeros((128, 64), dtype=np.float32)
    b2in = np.zeros((128, 64), dtype=np.float32)
    for r in range(8):
        e = (3 * r + 7 * hg) % 8
        w1in[:, r * 8192:(r + 1) * 8192] = np.transpose(
            w1[e].reshape(8, 128, 1024), (1, 0, 2)).reshape(128, 8192)
        w2in[:, r * 8192:(r + 1) * 8192] = np.transpose(
            w2[e].reshape(8, 128, 1024), (1, 0, 2)).reshape(128, 8192)
        b1in[:, r * 8:(r + 1) * 8] = b1[e].reshape(8, 128).T
        b2in[:, r * 8:(r + 1) * 8] = b2[e].reshape(8, 128).T
    return {
        "xin": xT,
        "wqin": _w_layout(Wq[hs]), "wkin": _w_layout(Wk[hs]),
        "wvin": _w_layout(Wv[hs]), "wgbin": wgb,
        "cwin": _cw_layout([c[hg * 512:(hg + 1) * 512] for c in p["cw"]]),
        "cbin": _cb_layout([c[hg * 512:(hg + 1) * 512] for c in p["cb"]]),
        "cin": Consts.host_tile(), "woin": woin,
        "sw1in": sw1l, "sw3in": sw3l, "sw2in": sw2l,
        "sb1in": sb1in, "sb3in": sb3in, "sb2in": sb2in,
        "w1in": w1in, "w2in": w2in, "b1in": b1in, "b2in": b2in,
    }


def kernel(**inputs):
    p = _parse_inputs(inputs)
    x = p["x"]
    xT = [np.ascontiguousarray(x[b].T) for b in range(B)]
    core_ins = [stage_core(p, xT[c // 2], c % 2) for c in range(8)]
    nc = _get_prog()
    res = run_bass_kernel_spmd(nc, core_ins, core_ids=list(range(8)))
    out = np.empty((B, T, D), dtype=np.float32)
    for b in range(B):
        full = np.concatenate([res.results[2 * b]["oout"],
                               res.results[2 * b + 1]["oout"]], axis=0)
        out[b] = full.T
    return out


def np_block_ref(x, p):
    from scipy.special import erf
    Tn = x.shape[0]
    xd = x.astype(np.float64)
    m = xd.mean(-1, keepdims=True)
    v = xd.var(-1, keepdims=True)
    xn = (xd - m) / np.sqrt(v + 1e-5)
    o_all = np.zeros((Tn, 1024))
    for hg in range(2):
        hs = slice(hg * 512, (hg + 1) * 512)
        zq = xn @ p["Wq"][hs].T.astype(np.float64)
        zk = xn @ p["Wk"][hs].T.astype(np.float64)
        zv = xn @ p["Wv"][hs].T.astype(np.float64)
        def conv(z, w, bc):
            zp = np.vstack([np.zeros((3, z.shape[1])), z])
            y = np.zeros_like(z)
            for j in range(4):
                y += w[None, hs, j] * zp[j:j + Tn, :]
            return y + bc[None, hs]
        q = conv(zq, p["cw"][0], p["cb"][0])
        k = conv(zk, p["cw"][1], p["cb"][1])
        vv = conv(zv, p["cw"][2], p["cb"][2])
        g = -np.log1p(np.exp(-(xn @ p["Wg"][hg * 8:(hg + 1) * 8].T.astype(np.float64))))
        beta = 1.0 / (1.0 + np.exp(-(xn @ p["Wb"][hg * 8:(hg + 1) * 8].T.astype(np.float64))))
        q = q.reshape(Tn, 8, 64)
        k = k.reshape(Tn, 8, 64)
        vv = vv.reshape(Tn, 8, 64)
        for h in range(8):
            o_all[:, hg * 512 + h * 64:hg * 512 + (h + 1) * 64] = np_delta_chunked(
                q[:, h], k[:, h], vv[:, h], g[:, h], beta[:, h])
    attn = o_all @ p["Wo"].T.astype(np.float64)
    h1 = xn @ p["sw1"].astype(np.float64) + p["sb1"].astype(np.float64)
    h3 = xn @ p["sw3"].astype(np.float64) + p["sb3"].astype(np.float64)
    shared = (h1 / (1 + np.exp(-h1)) * h3) @ p["sw2"].astype(np.float64) + p["sb2"].astype(np.float64)
    y = np.zeros((Tn, 1024))
    tok = np.arange(Tn)
    for kb in range(2):
        eids = (tok * 11 + kb * 7919) % 8
        for e in range(8):
            selm = eids == e
            he = xn[selm] @ p["w1"][e].astype(np.float64) + p["b1"][e].astype(np.float64)
            ge = 0.5 * he * (1 + erf(he / np.sqrt(2.0)))
            y[selm] += ge @ p["w2"][e].astype(np.float64) + p["b2"][e].astype(np.float64)
    y *= 0.5
    return xd + attn + (shared + y) / 2
